# version 08
# baseline (speedup 1.0000x reference)
import numpy as np
import ml_dtypes
import concourse.bass as bass
import concourse.mybir as mybir
from concourse.bass_utils import run_bass_kernel_spmd
from concourse.alu_op_type import AluOpType as AL

F32 = mybir.dt.float32
BF16 = mybir.dt.bfloat16
AF = mybir.ActivationFunctionType

L = 4096
NCORE = 8
RPC = 2048
PASSES = 2
NRC = 8
EPS = 1e-5


def _host_constants():
    j = np.arange(L, dtype=np.float64)
    rim = np.tan(np.pi * j / L)
    rim[L // 2] = 2e4
    a = np.arange(64, dtype=np.float64)
    C1 = np.cos(2 * np.pi * np.outer(a, a) / 64)
    S1 = np.sin(2 * np.pi * np.outer(a, a) / 64)
    T2c = np.cos(2 * np.pi * np.outer(a, a) / L)
    T2s = np.sin(2 * np.pi * np.outer(a, a) / L)
    return {
        "rimrow": rim.astype(np.float32)[None, :],
        "rimmat": rim.astype(np.float32).reshape(64, 64),
        "c1": C1.astype(np.float32),
        "s1": S1.astype(np.float32),
        "negs1": (-S1).astype(np.float32),
        "t2c": T2c.astype(np.float32),
        "t2s": T2s.astype(np.float32),
        "i64f": np.eye(64, dtype=np.float32),
        "i128b": np.eye(128, dtype=ml_dtypes.bfloat16),
        "ones64": np.ones((1, 64), dtype=np.float32),
        "ones128": np.ones((1, 128), dtype=np.float32),
    }


def _build(stage=5):
    nc = bass.Bass("TRN2", target_bir_lowering=False)
    D = nc.dram_tensor

    u_in = D("u", [RPC, L], F32, kind="ExternalInput")
    ins = {}
    for nm, shp in [("lamre", [64, 1]), ("lamim", [64, 1]), ("pre", [64, 1]), ("pim", [64, 1]),
                    ("qre", [64, 1]), ("qim", [64, 1]), ("vctre", [64, 64]), ("vctim", [64, 64]),
                    ("ctre", [64, 1]), ("ctim", [64, 1]), ("bp", [64, 1]), ("dd", [1, 1]),
                    ("lst", [1, 1]), ("biasbc", [128, L]), ("coefbc", [128, 256]),
                    ("scalbc", [128, 256])]:
        ins[nm] = D(nm, shp, F32, kind="ExternalInput")
    wt_in = D("wt", [L, L], F32, kind="ExternalInput")
    at_in = D("at", [L, 256], F32, kind="ExternalInput")
    bft_in = D("bftin", [256, L], F32, kind="ExternalInput")
    HC = _host_constants()
    cst = {}
    for k, v in HC.items():
        cst[k] = D("c_" + k, list(v.shape), BF16 if v.dtype == ml_dtypes.bfloat16 else F32,
                   kind="ExternalInput")

    kpad_d = D("kpad", [4608], F32, kind="Internal")
    ard_d = D("ard", [8, L], F32, kind="Internal")
    x2s_d = D("x2s", [RPC, L], F32, kind="Internal")
    ffns_d = D("ffns", [RPC, L], BF16, kind="Internal")
    y_out = D("y", [RPC, L], F32, kind="ExternalOutput")
    if stage < 5:
        dbg_k = D("dbg_k", [4608], F32, kind="ExternalOutput")
        dbg_a = D("dbg_a", [8, L], F32, kind="ExternalOutput")
        dbg_rre = D("dbg_rre", [64, L], F32, kind="ExternalOutput")
        dbg_rim = D("dbg_rim", [64, L], F32, kind="ExternalOutput")
        dbg_w64 = D("dbg_w64", [64, 672], F32, kind="ExternalOutput")
        dbg_x2 = D("dbg_x2", [128, L], F32, kind="ExternalOutput")
        dbg_f = D("dbg_f", [128, L], BF16, kind="ExternalOutput")
        dbg_xa = D("dbg_xa", [128, 2048], BF16, kind="ExternalOutput")
        dbg_gw = D("dbg_gw", [128, 16384], BF16, kind="ExternalOutput")
        dbg_atb = D("dbg_atb", [128, 8192], BF16, kind="ExternalOutput")
        dbg_bft = D("dbg_bft", [128, 8192], BF16, kind="ExternalOutput")
        dbg_xt = D("dbg_xt", [128, 32768], BF16, kind="ExternalOutput")

    from contextlib import ExitStack
    with ExitStack() as es:
        def sbt(name, shape, dt=F32):
            return es.enter_context(nc.sbuf_tensor(name, shape, dt))
        xt = sbt("xt", [128, 32 * 1024], BF16)
        gw = sbt("gw", [128, 16384], BF16)
        bft = sbt("bft", [128, 8192], BF16)
        atb = sbt("atb", [128, 8192], BF16)
        arena = sbt("arena", [128, 32768], BF16)
        stg = sbt("stg", [128, 512], BF16)
        i128b = sbt("i128b", [128, 128], BF16)
        i64f = sbt("i64f", [64, 64], F32)
        c1 = sbt("c1", [64, 64], F32)
        s1 = sbt("s1", [64, 64], F32)
        negs1 = sbt("negs1", [64, 64], F32)
        t2c = sbt("t2c", [64, 64], F32)
        t2s = sbt("t2s", [64, 64], F32)
        rimmat = sbt("rimmat", [64, 64], F32)
        ones64 = sbt("ones64", [1, 64], F32)
        ones128 = sbt("ones128", [1, 128], F32)
        lam_c = sbt("lam_c", [64, 2], F32)
        pq_c = sbt("pq_c", [64, 4], F32)
        ct_c = sbt("ct_c", [64, 2], F32)
        bp_c = sbt("bp_c", [64, 1], F32)
        vct = sbt("vct", [64, 128], F32)
        dl_c = sbt("dl_c", [1, 2], F32)
        w64 = sbt("w64", [64, 672], F32)
        kk = sbt("kk", [64, 704], F32)
        stats = sbt("stats", [128, 64], F32)
        glob = sbt("glob", [128, 2], F32)
        coefb = sbt("coefb", [128, 256], F32)
        scalb = sbt("scalb", [128, 256], F32)
        wbcast = sbt("wbcast", [128, 256], BF16)
        zeros4 = sbt("zeros4", [128, 64], F32)
        epsc = sbt("epsc", [128, 1], F32)
        v8m = sbt("v8m", [64, 16], F32)
        bnb = sbt("bnb", [128, 128], F32)

        pa0 = es.enter_context(nc.psum_tensor("pa0", [128, 512], F32))
        pa1 = es.enter_context(nc.psum_tensor("pa1", [128, 512], F32))
        pt0 = es.enter_context(nc.psum_tensor("pt0", [128, 512], F32))
        pt1 = es.enter_context(nc.psum_tensor("pt1", [128, 512], F32))
        pm = es.enter_context(nc.psum_tensor("pm", [128, 512], F32))

        SEM = {}
        for s in ("sQ", "gQ", "T", "V", "S"):
            SEM[s] = es.enter_context(nc.semaphore(s))

        Q = {e: [] for e in ("sync", "gpsimd", "tensor", "vector", "scalar")}
        C = {s: 0 for s in SEM}

        def wait(eng, sem, val):
            if val and val > 0:
                Q[eng].append(lambda E, s=SEM[sem], v=val: E.wait_ge(s, v))

        def dma(qeng, out_ap, in_ap, sem):
            C[sem] += 16
            Q[qeng].append(lambda E, o=out_ap, i=in_ap, s=SEM[sem]: E.dma_start(out=o, in_=i).then_inc(s, 16))
            return C[sem]

        def op(eng, fn, inc=None):
            if inc:
                C[inc] += 1
                Q[eng].append(lambda E, f=fn, s=SEM[inc]: f(E).then_inc(s, 1))
                return C[inc]
            Q[eng].append(lambda E, f=fn: f(E))
            return None

        def mkap(base_ap, pattern, offset):
            a = base_ap.copy()
            a.ap = mybir.VecI64Pair(pattern)
            a.offset = offset
            return a

        def af32(lo, hi, parts=128):
            return arena[0:parts, lo:hi].bitcast(F32)

        def ab16(lo, hi):
            return arena[:, lo:hi]

        ubuf = [af32(0, 8192), af32(8192, 16384)]
        ybuf = af32(16384, 24576)
        lnrev = ab16(24576, 28672)
        dump = ab16(24576, 28672)
        x2lnb = ab16(28672, 32768)
        x2in = [af32(16384, 24576), af32(0, 8192)]
        ffnin = [ab16(8192, 12288), ab16(12288, 16384)]
        biasb = ab16(28672, 32768)
        xawb = ab16(24576, 26624)
        xawT = ab16(26624, 28672)
        fdump = ab16(24576, 28672)
        rimrow_s = arena[0:1, 0:8192].bitcast(F32)
        tsrimb = af32(8192, 16384, parts=64)
        denim = af32(16384, 24576, parts=64)
        d2b = af32(24576, 32768, parts=64)
        karr = arena[0:8, 8192:16384].bitcast(F32)

        def wcol(i, n=1, parts=64):
            return w64[0:parts, i:i + n]
        XRE, XIM, YRE, YIM, ZRE, ZIM, ZRT, ZIT, KMT, TMP = [wcol(16 + 64 * i, 64) for i in range(10)]
        bre = wcol(656); bim = wcol(657); tscol = wcol(658)
        lre2 = wcol(659); nlre = wcol(660); ts11 = w64[0:1, 661:662]
        kslc = lambda i: kk[0:64, 64 * i:64 * (i + 1)]
        ARRE, ARIM = kslc(8), kslc(9)
        dcol = glob[:, 0:1]

        def stat(par, i):
            return stats[:, 32 * par + i: 32 * par + i + 1]

        AI, ASQ, ALN, AEX = AF.Identity, AF.Square, AF.Ln, AF.Exp

        for dst, src in [(lam_c[:, 0:1], ins["lamre"]), (lam_c[:, 1:2], ins["lamim"]),
                         (pq_c[:, 0:1], ins["pre"]), (pq_c[:, 1:2], ins["pim"]),
                         (pq_c[:, 2:3], ins["qre"]), (pq_c[:, 3:4], ins["qim"]),
                         (ct_c[:, 0:1], ins["ctre"]), (ct_c[:, 1:2], ins["ctim"]),
                         (bp_c[:], ins["bp"]), (vct[:, 0:64], ins["vctre"]),
                         (vct[:, 64:128], ins["vctim"]),
                         (dl_c[:, 0:1], ins["dd"]), (dl_c[:, 1:2], ins["lst"]),
                         (i128b[:], cst["i128b"]), (i64f[:], cst["i64f"]),
                         (c1[:], cst["c1"]), (s1[:], cst["s1"]), (negs1[:], cst["negs1"]),
                         (t2c[:], cst["t2c"]), (t2s[:], cst["t2s"]),
                         (rimmat[:], cst["rimmat"]), (ones64[:], cst["ones64"]),
                         (ones128[:], cst["ones128"]), (rimrow_s, cst["rimrow"]),
                         (coefb[:], ins["coefbc"]), (scalb[:], ins["scalbc"])]:
            dma("sync", dst, src.ap(), "sQ")
        ld_all = C["sQ"]
        dma("gpsimd", bft[:], mkap(bft_in.ap(), [[4096, 128], [524288, 2], [1, 4096]], 0), "gQ")
        gq_wload = dma("gpsimd", atb[:], mkap(at_in.ap(), [[256, 128], [32768, 32], [1, 256]], 0), "gQ")

        wait("scalar", "sQ", ld_all)
        op("scalar", lambda E: E.activation(out=ts11, in_=dl_c[0:1, 1:2], func=AEX, scale=-1.0))
        op("scalar", lambda E: E.mul(ts11, ts11, 2.0))
        op("scalar", lambda E: E.activation(out=lre2, in_=lam_c[:, 0:1], func=ASQ))
        op("scalar", lambda E: E.mul(nlre, lam_c[:, 0:1], -1.0))
        op("scalar", lambda E: E.drain())
        s_prep = op("scalar", lambda E: E.nop(), inc="S")
        wait("vector", "sQ", ld_all)
        op("vector", lambda E: E.memset(zeros4[:], 0.0))
        op("vector", lambda E: E.memset(epsc[:], EPS))
        v_zw = op("vector", lambda E: E.tensor_tensor(out=wbcast[:], in0=coefb[:], in1=scalb[:], op=AL.mult), inc="V")

        wait("tensor", "S", s_prep)
        wait("tensor", "sQ", ld_all)
        op("tensor", lambda E: E.matmul(pm[0:64, 0:1], ones64[:], ts11, start=True, stop=True))
        op("tensor", lambda E: E.matmul(pm[0:128, 1:2], ones128[:], dl_c[0:1, 0:1], start=True, stop=True))
        op("tensor", lambda E: E.matmul(pm[0:64, 2:3], vct[:, 0:64], bp_c[:], start=True, stop=True))
        t_b = op("tensor", lambda E: E.matmul(pm[0:64, 3:4], vct[:, 64:128], bp_c[:], start=True, stop=True), inc="T")
        wait("scalar", "T", t_b)
        op("scalar", lambda E: E.copy(tscol, pm[0:64, 0:1]))
        op("scalar", lambda E: E.copy(dcol, pm[0:128, 1:2]))
        op("scalar", lambda E: E.copy(bre, pm[0:64, 2:3]))
        op("scalar", lambda E: E.copy(bim, pm[0:64, 3:4]))
        nBre = wcol(663); nPre = wcol(664)
        pA = wcol(665); pB = wcol(666)
        pAB = w64[0:64, 665:667]
        pDump = w64[0:64, 667:669]
        op("scalar", lambda E: E.mul(nBre, bre, -1.0))
        op("scalar", lambda E: E.mul(nPre, pq_c[:, 0:1], -1.0))
        pairs = [(ct_c[:, 0:1], ct_c[:, 1:2], bre, bim, nBre),
                 (ct_c[:, 0:1], ct_c[:, 1:2], pq_c[:, 0:1], pq_c[:, 1:2], nPre),
                 (pq_c[:, 2:3], pq_c[:, 3:4], bre, bim, nBre),
                 (pq_c[:, 2:3], pq_c[:, 3:4], pq_c[:, 0:1], pq_c[:, 1:2], nPre)]
        for k2, (are, aim, br_, bi_, nbr_) in enumerate(pairs):
            cre, cim = 2 * k2, 2 * k2 + 1
            op("scalar", lambda E, a=are, b=br_: E.activation(out=pA, in_=a, func=AI, scale=b))
            op("scalar", lambda E, a=aim, b=bi_: E.activation(out=pB, in_=a, func=AI, scale=b))
            op("scalar", lambda E, c=cre: E.activation(out=pDump, in_=pAB, func=AI, accum_out=v8m[:, c:c + 1]))
            op("scalar", lambda E, a=are, b=bi_: E.activation(out=pA, in_=a, func=AI, scale=b))
            op("scalar", lambda E, a=aim, b=nbr_: E.activation(out=pB, in_=a, func=AI, scale=b))
            op("scalar", lambda E, c=cim: E.activation(out=pDump, in_=pAB, func=AI, accum_out=v8m[:, c:c + 1]))
            op("scalar", lambda E, d=8 + 2 * k2, c=cim: E.mul(v8m[:, d:d + 1], v8m[:, c:c + 1], -1.0))
            op("scalar", lambda E, d=9 + 2 * k2, c=cre: E.copy(v8m[:, d:d + 1], v8m[:, c:c + 1]))
        op("scalar", lambda E: E.drain())
        s_v8 = op("scalar", lambda E: E.nop(), inc="S")

        s_tsr = s_v8
        for cix in range(8):
            wait("tensor", "S", s_tsr)
            t_n = op("tensor", lambda E, c=cix: E.matmul(pm[0:64, 0:512], ones64[:], rimrow_s[0:1, 512 * c:512 * (c + 1)], start=True, stop=True), inc="T")
            wait("scalar", "T", t_n)
            s_tsr = op("scalar", lambda E, c=cix: E.activation(out=tsrimb[:, 512 * c:512 * (c + 1)], in_=pm[0:64, 0:512], func=AI, scale=tscol), inc="S")
        wait("vector", "S", s_tsr)
        op("vector", lambda E: E.tensor_scalar(out=denim, in0=tsrimb, scalar1=lam_c[:, 1:2], scalar2=None, op0=AL.subtract))
        op("vector", lambda E: E.tensor_tensor(out=d2b, in0=denim, in1=denim, op=AL.mult))
        v_d2 = op("vector", lambda E: E.tensor_scalar(out=d2b, in0=d2b, scalar1=lre2, scalar2=None, op0=AL.add), inc="V")
        wait("scalar", "V", v_d2)
        op("scalar", lambda E: E.activation(out=tsrimb, in_=d2b, func=ALN))
        s_rcp = op("scalar", lambda E: E.activation(out=tsrimb, in_=tsrimb, func=AEX, scale=-1.0), inc="S")
        wait("vector", "S", s_rcp)
        op("vector", lambda E: E.tensor_scalar(out=d2b, in0=tsrimb, scalar1=nlre, scalar2=None, op0=AL.mult))
        v_r = op("vector", lambda E: E.scalar_tensor_tensor(out=denim, in0=denim, scalar=-1.0, in1=tsrimb, op0=AL.mult, op1=AL.mult), inc="V")
        wait("tensor", "V", v_r)
        v_kev = 0
        for cix in range(8):
            wait("tensor", "V", v_kev)
            op("tensor", lambda E, c=cix: E.matmul(pm[0:8, 0:512], v8m[:, 0:8], d2b[:, 512 * c:512 * (c + 1)], start=True, stop=False))
            t_n = op("tensor", lambda E, c=cix: E.matmul(pm[0:8, 0:512], v8m[:, 8:16], denim[:, 512 * c:512 * (c + 1)], start=False, stop=True), inc="T")
            wait("vector", "T", t_n)
            v_kev = op("vector", lambda E, c=cix: E.tensor_copy(karr[:, 512 * c:512 * (c + 1)], pm[0:8, 0:512]), inc="V")
        wait("sync", "V", v_kev)
        sq_karr = dma("sync", ard_d.ap(), karr, "sQ")
        wait("sync", "sQ", sq_karr)
        n = 0
        for comp in range(8):
            n = dma("sync", kslc(comp), mkap(ard_d.ap(), [[64, 64], [1, 64]], comp * 4096), "sQ")
        wait("vector", "sQ", n)
        K00R, K00I, K01R, K01I, K10R, K10I, K11R, K11I = [kslc(i) for i in range(8)]
        TMP2 = kslc(10)
        op("vector", lambda E: E.tensor_tensor(out=ARRE, in0=K01R, in1=K10R, op=AL.mult))
        op("vector", lambda E: E.tensor_tensor(out=ARIM, in0=K01I, in1=K10I, op=AL.mult))
        op("vector", lambda E: E.tensor_tensor(out=ARRE, in0=ARRE, in1=ARIM, op=AL.subtract))
        op("vector", lambda E: E.tensor_tensor(out=ARIM, in0=K01R, in1=K10I, op=AL.mult))
        op("vector", lambda E: E.tensor_tensor(out=K01R, in0=K01I, in1=K10R, op=AL.mult))
        op("vector", lambda E: E.tensor_tensor(out=ARIM, in0=ARIM, in1=K01R, op=AL.add))
        op("vector", lambda E: E.tensor_scalar(out=K11R, in0=K11R, scalar1=1.0, scalar2=None, op0=AL.add))
        op("vector", lambda E: E.tensor_tensor(out=K01R, in0=K11R, in1=K11R, op=AL.mult))
        op("vector", lambda E: E.tensor_tensor(out=K01I, in0=K11I, in1=K11I, op=AL.mult))
        v_den = op("vector", lambda E: E.tensor_tensor(out=K01R, in0=K01R, in1=K01I, op=AL.add), inc="V")
        wait("scalar", "V", v_den)
        op("scalar", lambda E: E.activation(out=TMP2, in_=K01R, func=ALN))
        s_rcp2 = op("scalar", lambda E: E.activation(out=K01I, in_=TMP2, func=AEX, scale=-1.0), inc="S")
        wait("vector", "S", s_rcp2)
        op("vector", lambda E: E.tensor_tensor(out=K10R, in0=ARRE, in1=K11R, op=AL.mult))
        op("vector", lambda E: E.tensor_tensor(out=TMP2, in0=ARIM, in1=K11I, op=AL.mult))
        op("vector", lambda E: E.tensor_tensor(out=K10R, in0=K10R, in1=TMP2, op=AL.add))
        op("vector", lambda E: E.tensor_tensor(out=K10R, in0=K10R, in1=K01I, op=AL.mult))
        op("vector", lambda E: E.tensor_tensor(out=TMP2, in0=ARIM, in1=K11R, op=AL.mult))
        op("vector", lambda E: E.tensor_tensor(out=K11I, in0=ARRE, in1=K11I, op=AL.mult))
        op("vector", lambda E: E.tensor_tensor(out=TMP2, in0=TMP2, in1=K11I, op=AL.subtract))
        op("vector", lambda E: E.tensor_tensor(out=TMP2, in0=TMP2, in1=K01I, op=AL.mult))
        op("vector", lambda E: E.tensor_tensor(out=K00R, in0=K00R, in1=K10R, op=AL.subtract))
        op("vector", lambda E: E.tensor_tensor(out=K00I, in0=K00I, in1=TMP2, op=AL.subtract))
        op("vector", lambda E: E.tensor_tensor(out=ARRE, in0=rimmat[:], in1=K00I, op=AL.mult))
        op("vector", lambda E: E.tensor_tensor(out=ARRE, in0=K00R, in1=ARRE, op=AL.subtract))
        op("vector", lambda E: E.tensor_tensor(out=ARIM, in0=rimmat[:], in1=K00R, op=AL.mult))
        v_ar = op("vector", lambda E: E.tensor_tensor(out=ARIM, in0=ARIM, in1=K00I, op=AL.add), inc="V")
        wait("tensor", "V", v_ar)
        op("tensor", lambda E: E.matmul(pm[0:64, 0:64], c1[:], ARRE, start=True, stop=False))
        op("tensor", lambda E: E.matmul(pm[0:64, 0:64], negs1[:], ARIM, start=False, stop=True))
        op("tensor", lambda E: E.matmul(pm[0:64, 64:128], s1[:], ARRE, start=True, stop=False))
        t_n = op("tensor", lambda E: E.matmul(pm[0:64, 64:128], c1[:], ARIM, start=False, stop=True), inc="T")
        wait("vector", "T", t_n)
        op("vector", lambda E: E.tensor_copy(YRE, pm[0:64, 0:64]))
        op("vector", lambda E: E.tensor_copy(YIM, pm[0:64, 64:128]))
        op("vector", lambda E: E.tensor_tensor(out=ZRE, in0=YRE, in1=t2c[:], op=AL.mult))
        op("vector", lambda E: E.tensor_tensor(out=TMP, in0=YIM, in1=t2s[:], op=AL.mult))
        op("vector", lambda E: E.tensor_tensor(out=ZRE, in0=ZRE, in1=TMP, op=AL.subtract))
        op("vector", lambda E: E.tensor_tensor(out=ZIM, in0=YRE, in1=t2s[:], op=AL.mult))
        op("vector", lambda E: E.tensor_tensor(out=TMP, in0=YIM, in1=t2c[:], op=AL.mult))
        v_tw = op("vector", lambda E: E.tensor_tensor(out=ZIM, in0=ZIM, in1=TMP, op=AL.add), inc="V")
        wait("tensor", "V", v_tw)
        op("tensor", lambda E: E.matmul(pm[0:64, 128:192], ZRE, i64f[:], start=True, stop=True))
        t_n = op("tensor", lambda E: E.matmul(pm[0:64, 192:256], ZIM, i64f[:], start=True, stop=True), inc="T")
        wait("vector", "T", t_n)
        op("vector", lambda E: E.tensor_copy(ZRT, pm[0:64, 128:192]))
        v_n = op("vector", lambda E: E.tensor_copy(ZIT, pm[0:64, 192:256]), inc="V")
        wait("tensor", "V", v_n)
        op("tensor", lambda E: E.matmul(pm[0:64, 256:320], ZRT, c1[:], start=True, stop=False))
        t_n = op("tensor", lambda E: E.matmul(pm[0:64, 256:320], ZIT, negs1[:], start=False, stop=True), inc="T")
        wait("vector", "T", t_n)
        v_n = op("vector", lambda E: E.tensor_copy(KMT, pm[0:64, 256:320]), inc="V")
        wait("tensor", "V", v_n)
        t_kprep_end = op("tensor", lambda E: E.matmul(pm[0:64, 320:384], KMT, i64f[:], start=True, stop=True), inc="T")
        wait("scalar", "T", t_kprep_end)
        s_km = op("scalar", lambda E: E.activation(out=TMP, in_=pm[0:64, 320:384], func=AI, scale=1.0 / 4096.0), inc="S")
        wait("sync", "S", s_km)
        wait("sync", "V", v_zw)
        dma("sync", mkap(kpad_d.ap(), [[4, 128], [1, 4]], 0), zeros4[:, 0:4], "sQ")
        sq_kpad = dma("sync", mkap(kpad_d.ap(), [[64, 64], [1, 64]], 512), TMP, "sQ")

        def emit_g_load():
            wait("gpsimd", "sQ", sq_kpad)
            last = None
            for d_ in range(8):
                last = dma("gpsimd", gw[:, 2048 * d_:2048 * (d_ + 1)],
                           mkap(kpad_d.ap(), [[1, 128], [128, 4], [1, 512]], 512 * d_ + 1), "gQ")
            return last

        gq_g = emit_g_load()

        t_pt_free = {0: 0, 1: 0}
        t_pa_free = {0: 0, 1: 0}
        u_free = {0: 0, 1: 0}
        sq_x2store = 0
        t_x2trans = {}
        state = {"pmfree": 0}

        def emit_lnstats(par, x_ap, dmp_ap):
            b = 16 * par
            col = lambda i, n=1: bnb[:, b + i:b + i + n]
            cQ, cS, cM = col(0), col(1), col(2)
            op("vector", lambda E: E.tensor_tensor(out=dmp_ap, in0=x_ap, in1=x_ap, op=AL.mult))
            op("vector", lambda E: E.tensor_reduce(out=cQ, in_=dmp_ap, axis=mybir.AxisListType.X, op=AL.add))
            op("vector", lambda E: E.tensor_reduce(out=cS, in_=x_ap, axis=mybir.AxisListType.X, op=AL.add))
            op("vector", lambda E: E.drain())
            v_st = op("vector", lambda E: E.nop(), inc="V")
            wait("scalar", "V", v_st)
            op("scalar", lambda E: E.mul(cM, cS, 1.0 / L))
            op("scalar", lambda E: E.activation(out=col(3), in_=cM, func=ASQ))
            op("scalar", lambda E: E.mul(col(4), cQ, 1.0 / L))
            op("scalar", lambda E: E.activation(out=col(5), in_=col(3), func=AI, scale=-1.0, bias=epsc[:]))
            op("scalar", lambda E: E.activation(out=col(10, 2), in_=col(4, 2), func=AI, accum_out=col(6)))
            op("scalar", lambda E: E.activation(out=col(7), in_=col(6), func=ALN))
            op("scalar", lambda E: E.activation(out=col(8), in_=col(7), func=AEX, scale=-0.5))
            op("scalar", lambda E: E.drain())
            s_done = op("scalar", lambda E: E.nop(), inc="S")
            return s_done, cM, col(8)

        def conv_rc(p, rc):
            nonlocal sq_x2store
            grc = p * NRC + rc
            par = grc % 2
            row = grc * 128
            wait("sync", "V", u_free[par])
            if grc <= 1:
                wait("sync", "T", t_kprep_end)
                if grc == 1:
                    wait("sync", "sQ", sq_karr)
            sq_u = dma("sync", ubuf[par], u_in.ap()[row:row + 128, :], "sQ")
            wait("vector", "sQ", sq_u)
            s_ln1, m1, sc1 = emit_lnstats(par, ubuf[par], dump[:])
            wait("vector", "S", s_ln1)
            if grc == 0:
                wait("vector", "T", t_kprep_end)
            v_ln = op("vector", lambda E, m=m1, s=sc1: E.tensor_scalar(
                out=lnrev[:], in0=ubuf[par][:, ::-1], scalar1=m, scalar2=s,
                op0=AL.subtract, op1=AL.mult), inc="V")
            wait("tensor", "V", v_ln)
            v_xtrdy = 0
            for g in range(8):
                bank = [pt0, pt1][g % 2]
                wait("tensor", "V", t_pt_free[g % 2])
                t_g = None
                for kp in range(4):
                    b_blk = 4 * g + 3 - kp
                    t_g = op("tensor", lambda E, b=b_blk, k=kp, bk=bank: E.matmul(
                        bk[:, 128 * k:128 * (k + 1)], lnrev[:, 128 * b:128 * (b + 1)], i128b[:],
                        start=True, stop=True), inc=("T" if kp == 3 else None))
                wait("vector", "T", t_g)
                dst = mkap(xt[:], [[32768, 128], [1024, 4], [1, 128]], (28 - 4 * g) * 1024 + rc * 128)
                srcv = mkap(bank[:], [[512, 128], [128, 4], [1, 128]], 0)
                t_pt_free[g % 2] = op("vector", lambda E, d=dst, s=srcv: E.tensor_copy(d, s), inc="V")
                v_xtrdy = t_pt_free[g % 2]
            wait("tensor", "V", v_xtrdy)
            if grc == 0:
                wait("tensor", "gQ", gq_g)
            v_yb = 0
            for i in range(8):
                bank = [pa0, pa1][i % 2]
                wait("tensor", "V", t_pa_free[i % 2])
                nmm = 4 * (i + 1)
                cnt = 0
                t_i = None
                for j in range(i + 1):
                    d_ = i - j
                    for bt in range(4):
                        cnt += 1
                        sl = 4 * j + bt
                        t_i = op("tensor", lambda E, s_=sl, dd_=d_, b=bt, bk=bank, st=(cnt == 1), sp=(cnt == nmm): E.matmul(
                            bk[:, 0:512], xt[:, s_ * 1024 + rc * 128: s_ * 1024 + rc * 128 + 128],
                            gw[:, dd_ * 2048 + (3 - b) * 512: dd_ * 2048 + (3 - b) * 512 + 512],
                            start=st, stop=sp), inc=("T" if cnt == nmm else None))
                wait("vector", "T", t_i)
                if i == 0:
                    wait("vector", "sQ", sq_x2store)
                stop_ = 3583 - 512 * i
                sl_ln = lnrev[:, 4095 - 512 * i: (stop_ if stop_ >= 0 else None): -1]
                t_pa_free[i % 2] = op("vector", lambda E, b=bank, s=sl_ln, i_=i: E.scalar_tensor_tensor(
                    out=ybuf[:, 512 * i_:512 * (i_ + 1)], in0=s, scalar=dcol, in1=b[:, 0:512],
                    op0=AL.mult, op1=AL.add), inc="V")
                v_yb = t_pa_free[i % 2]
            s_pl, m2, sc2 = emit_lnstats(par, ybuf, dump[:])
            wait("vector", "S", s_pl)
            op("vector", lambda E, m=m2, s=sc2: E.tensor_scalar(
                out=ybuf, in0=ybuf, scalar1=m, scalar2=s, op0=AL.subtract, op1=AL.mult))
            v_x2 = op("vector", lambda E: E.tensor_tensor(out=ybuf, in0=ybuf, in1=ubuf[par], op=AL.add), inc="V")
            u_free[par] = v_x2
            s_l2, m3, sc3 = emit_lnstats(par, ybuf, dump[:])
            wait("vector", "S", s_l2)
            if grc >= 1:
                wait("vector", "T", t_x2trans.get(grc - 1, 0))
            v_x2ln = op("vector", lambda E, m=m3, s=sc3: E.tensor_scalar(
                out=x2lnb[:], in0=ybuf, scalar1=m, scalar2=s,
                op0=AL.subtract, op1=AL.mult), inc="V")
            wait("sync", "V", v_x2ln)
            sq_x2store = dma("sync", x2s_d.ap()[row:row + 128, :], ybuf, "sQ")
            wait("tensor", "V", v_x2ln)
            t_last = None
            for g in range(8):
                bank = [pt0, pt1][g % 2]
                wait("tensor", "V", t_pt_free[g % 2])
                t_g = None
                for kp in range(4):
                    b_blk = 4 * g + kp
                    t_g = op("tensor", lambda E, b=b_blk, k=kp, bk=bank: E.matmul(
                        bk[:, 128 * k:128 * (k + 1)], x2lnb[:, 128 * b:128 * (b + 1)], i128b[:],
                        start=True, stop=True), inc=("T" if kp == 3 else None))
                t_last = t_g
                wait("vector", "T", t_g)
                dst = mkap(xt[:], [[32768, 128], [1024, 4], [1, 128]], (4 * g) * 1024 + rc * 128)
                srcv = mkap(bank[:], [[512, 128], [128, 4], [1, 128]], 0)
                t_pt_free[g % 2] = op("vector", lambda E, d=dst, s=srcv: E.tensor_copy(d, s), inc="V")
            t_x2trans[grc] = t_last
            return t_last

        t_ffn_end = 0
        v_ffnend = 0
        s_ffnend = 0
        npass = 0 if stage <= 1 else (1 if stage < 5 else PASSES)
        for p in range(npass):
            if p > 0:
                wait("gpsimd", "T", t_ffn_end)
                gq_gload = emit_g_load()
                sq_x2store = C["sQ"]
                wait("sync", "V", v_ffnend)
                wait("sync", "S", s_ffnend)
                wait("sync", "sQ", C["sQ"])
                wait("scalar", "V", v_ffnend)
                wait("vector", "S", s_ffnend)
                wait("vector", "T", t_ffn_end)
                wait("tensor", "V", v_ffnend)
                wait("tensor", "gQ", gq_gload)
            t_convdone = 0
            for rc in range(NRC):
                t_convdone = conv_rc(p, rc)
            v_convdone = C["V"]
            s_convdone = C["S"]
            sq_convdone = C["sQ"]

            if stage < 3:
                continue
            wait("gpsimd", "T", t_convdone)
            wait("gpsimd", "V", v_convdone)
            wait("gpsimd", "S", s_convdone)
            gq_bias = dma("gpsimd", biasb[:], ins["biasbc"].ap(), "gQ")
            if p == 0:
                wait("tensor", "gQ", gq_wload)
            v_xaw = {}
            for rc in range(NRC):
                wait("tensor", "V", state["pmfree"])
                t_xa = None
                for dt_ in range(32):
                    t_xa = op("tensor", lambda E, d=dt_, r=rc, st=(dt_ == 0), sp=(dt_ == 31): E.matmul(
                        pm[:, 0:256], xt[:, d * 1024 + r * 128: d * 1024 + r * 128 + 128],
                        atb[:, d * 256:(d + 1) * 256], start=st, stop=sp), inc=("T" if dt_ == 31 else None))
                wait("vector", "T", t_xa)
                if rc == 0:
                    wait("vector", "S", s_convdone)
                state["pmfree"] = op("vector", lambda E, r=rc: E.tensor_tensor(
                    out=xawb[:, r * 256:(r + 1) * 256], in0=pm[:, 0:256], in1=wbcast[:], op=AL.mult), inc="V")
                v_xaw[rc] = state["pmfree"]
            for rc in range(NRC):
                wait("tensor", "V", v_xaw[rc])
                for t_ in range(2):
                    bank = [pt0, pt1][t_]
                    wait("tensor", "V", t_pt_free[t_])
                    t_g = op("tensor", lambda E, r=rc, t=t_, bk=bank: E.matmul(
                        bk[:, 0:128], xawb[:, r * 256 + t * 128: r * 256 + t * 128 + 128], i128b[:],
                        start=True, stop=True), inc="T")
                    wait("vector", "T", t_g)
                    t_pt_free[t_] = op("vector", lambda E, r=rc, t=t_, bk=bank: E.tensor_copy(
                        xawT[:, t * 1024 + r * 128: t * 1024 + r * 128 + 128], bk[:, 0:128]), inc="V")
            t_slice_mm = {}
            gq_stg_free = {0: 0, 1: 0}
            for k in range(16):
                wait("gpsimd", "T", t_slice_mm.get(k - 2, t_convdone))
                gq_wt = dma("gpsimd", gw[:, (k % 2) * 8192:(k % 2) * 8192 + 8192],
                            mkap(wt_in.ap(), [[4096, 128], [524288, 32], [1, 256]], k * 256), "gQ")
                wait("tensor", "gQ", gq_wt)
                if k == 0:
                    wait("tensor", "gQ", gq_bias)
                for rc in range(NRC):
                    bank = [pa0, pa1][rc % 2]
                    par = rc % 2
                    row = (p * NRC + rc) * 128
                    wait("tensor", "V", t_pa_free[par])
                    for dt_ in range(32):
                        op("tensor", lambda E, d=dt_, r=rc, kk_=k, bk=bank, st=(dt_ == 0): E.matmul(
                            bk[:, 0:256], xt[:, d * 1024 + r * 128: d * 1024 + r * 128 + 128],
                            gw[:, (kk_ % 2) * 8192 + d * 256:(kk_ % 2) * 8192 + (d + 1) * 256],
                            start=st, stop=False))
                    t_m = None
                    for t_ in range(2):
                        t_m = op("tensor", lambda E, t=t_, r=rc, kk_=k, bk=bank: E.matmul(
                            bk[:, 0:256], xawT[:, t * 1024 + r * 128: t * 1024 + r * 128 + 128],
                            bft[:, t * 4096 + kk_ * 256: t * 4096 + (kk_ + 1) * 256],
                            start=False, stop=(t == 1)), inc=("T" if t_ == 1 else None))
                    if rc == NRC - 1:
                        t_slice_mm[k] = t_m
                    wait("vector", "T", t_m)
                    wait("vector", "gQ", gq_stg_free[par])
                    v_ev = op("vector", lambda E, kk_=k, bk=bank, pp=par: E.tensor_tensor(
                        out=stg[:, pp * 256:(pp + 1) * 256], in0=bk[:, 0:256],
                        in1=biasb[:, kk_ * 256:(kk_ + 1) * 256], op=AL.add), inc="V")
                    t_pa_free[par] = v_ev
                    wait("gpsimd", "V", v_ev)
                    gq_stg_free[par] = dma("gpsimd", ffns_d.ap()[row:row + 128, k * 256:(k + 1) * 256],
                                           stg[:, par * 256:(par + 1) * 256], "gQ")
            t_ffn_end = t_slice_mm[15]
            gq_allst = C["gQ"]
            if stage < 4:
                continue
            v_fin = {0: C["V"], 1: C["V"]}
            tbig = af32(24576, 32768)
            v_first = True
            for rc in range(NRC):
                par = rc % 2
                row = (p * NRC + rc) * 128
                wait("sync", "V", v_fin[par])
                wait("sync", "sQ", sq_convdone)
                sq_x2 = dma("sync", x2in[par], x2s_d.ap()[row:row + 128, :], "sQ")
                wait("gpsimd", "gQ", gq_allst)
                wait("gpsimd", "V", v_fin[par])
                gq_f = dma("gpsimd", ffnin[par], ffns_d.ap()[row:row + 128, :], "gQ")
                wait("vector", "gQ", gq_f)
                if v_first:
                    wait("vector", "T", t_ffn_end)
                    v_first = False
                s_f, mf, sf = emit_lnstats(par, ffnin[par], fdump[:])
                wait("vector", "S", s_f)
                wait("vector", "sQ", sq_x2)
                op("vector", lambda E, pp=par, m=mf, s=sf: E.tensor_scalar(
                    out=tbig, in0=ffnin[pp], scalar1=m, scalar2=s,
                    op0=AL.subtract, op1=AL.mult))
                v_done = op("vector", lambda E, pp=par: E.tensor_tensor(
                    out=x2in[pp], in0=x2in[pp], in1=tbig, op=AL.add), inc="V")
                v_fin[par] = v_done
                wait("sync", "V", v_done)
                dma("sync", y_out.ap()[row:row + 128, :], x2in[par], "sQ")
            v_ffnend = C["V"]
            s_ffnend = C["S"]

        if stage < 5:
            wait("sync", "sQ", C["sQ"])
            dma("sync", dbg_k.ap(), kpad_d.ap(), "sQ")
            dma("sync", dbg_a.ap(), ard_d.ap(), "sQ")
            if stage == 1:
                dma("sync", dbg_rre.ap(), d2b, "sQ")
                dma("sync", dbg_rim.ap(), denim, "sQ")
                dma("sync", dbg_w64.ap(), w64[:], "sQ")
            if stage >= 2:
                dma("sync", dbg_x2.ap(), x2s_d.ap()[0:128, :], "sQ")
            if stage >= 3:
                wait("sync", "gQ", C["gQ"])
                dma("sync", dbg_f.ap(), ffns_d.ap()[0:128, :], "sQ")
                dma("sync", dbg_xa.ap(), xawb, "sQ")
                dma("sync", dbg_gw.ap(), gw[:], "sQ")
                dma("sync", dbg_atb.ap(), atb[:], "sQ")
                dma("sync", dbg_bft.ap(), bft[:], "sQ")
                dma("sync", dbg_xt.ap(), xt[:], "sQ")
        wait("sync", "sQ", C["sQ"])

        with nc.Block() as block:
            @block.sync
            def _(E):
                for fn in Q["sync"]:
                    fn(E)

            @block.gpsimd
            def _(E):
                for fn in Q["gpsimd"]:
                    fn(E)

            @block.tensor
            def _(E):
                for fn in Q["tensor"]:
                    fn(E)

            @block.vector
            def _(E):
                for fn in Q["vector"]:
                    fn(E)

            @block.scalar
            def _(E):
                for fn in Q["scalar"]:
                    fn(E)

    return nc


def _prep_common(L0_attn_Lambda_ri, L0_attn_p_ri, L0_attn_q_ri, L0_attn_Vc_ri,
                 L0_attn_Ct_ri, L0_attn_Bp, L0_attn_D, L0_attn_log_step, L0_ffn_W,
                 L0_ffn_bias, L0_ffn_A, L0_ffn_Bf, L0_ffn_coeffs, L0_ffn_scales):
    f32 = np.float32
    lam = np.asarray(L0_attn_Lambda_ri, f32)
    pp = np.asarray(L0_attn_p_ri, f32)
    qq = np.asarray(L0_attn_q_ri, f32)
    vc = np.asarray(L0_attn_Vc_ri, f32)
    ct = np.asarray(L0_attn_Ct_ri, f32).reshape(64, 2)
    common = {
        "lamre": np.ascontiguousarray(lam[:, 0:1]),
        "lamim": np.ascontiguousarray(lam[:, 1:2]),
        "pre": np.ascontiguousarray(pp[:, 0:1]),
        "pim": np.ascontiguousarray(pp[:, 1:2]),
        "qre": np.ascontiguousarray(qq[:, 0:1]),
        "qim": np.ascontiguousarray(qq[:, 1:2]),
        "vctre": np.ascontiguousarray(vc[:, :, 0].T),
        "vctim": np.ascontiguousarray(vc[:, :, 1].T),
        "ctre": np.ascontiguousarray(ct[:, 0:1]),
        "ctim": np.ascontiguousarray(ct[:, 1:2]),
        "bp": np.ascontiguousarray(np.asarray(L0_attn_Bp, f32)),
        "dd": np.asarray(L0_attn_D, f32).reshape(1, 1),
        "lst": np.asarray(L0_attn_log_step, f32).reshape(1, 1),
        "wt": np.ascontiguousarray(np.asarray(L0_ffn_W, f32).T),
        "at": np.ascontiguousarray(np.asarray(L0_ffn_A, f32).reshape(256, L).T),
        "bftin": np.ascontiguousarray(np.transpose(np.asarray(L0_ffn_Bf, f32), (0, 2, 1)).reshape(256, L)),
        "biasbc": np.ascontiguousarray(np.broadcast_to(np.asarray(L0_ffn_bias, f32)[None, :], (128, L))),
        "coefbc": np.ascontiguousarray(np.broadcast_to(
            np.asarray(L0_ffn_coeffs, f32)[np.repeat(np.arange(16), 16)][None, :], (128, 256))),
        "scalbc": np.ascontiguousarray(np.broadcast_to(
            np.asarray(L0_ffn_scales, f32)[np.repeat(np.arange(16), 16)][None, :], (128, 256))),
    }
    for k, v in _host_constants().items():
        common["c_" + k] = v
    return common


def kernel(L0_attn_u, L0_attn_Lambda_ri, L0_attn_p_ri, L0_attn_q_ri, L0_attn_Vc_ri,
           L0_attn_Ct_ri, L0_attn_Bp, L0_attn_D, L0_attn_log_step, L0_ffn_W,
           L0_ffn_bias, L0_ffn_A, L0_ffn_Bf, L0_ffn_coeffs, L0_ffn_scales):
    u = np.ascontiguousarray(np.asarray(L0_attn_u, np.float32).reshape(16384, L))
    common = _prep_common(L0_attn_Lambda_ri, L0_attn_p_ri, L0_attn_q_ri, L0_attn_Vc_ri,
                          L0_attn_Ct_ri, L0_attn_Bp, L0_attn_D, L0_attn_log_step,
                          L0_ffn_W, L0_ffn_bias, L0_ffn_A, L0_ffn_Bf, L0_ffn_coeffs,
                          L0_ffn_scales)
    nc = _build()
    in_maps = []
    for c in range(NCORE):
        m = dict(common)
        m["u"] = np.ascontiguousarray(u[c * RPC:(c + 1) * RPC])
        in_maps.append(m)
    res = run_bass_kernel_spmd(nc, in_maps, core_ids=list(range(NCORE)))
    out = np.concatenate([np.asarray(r["y"]) for r in res.results], axis=0)
    return np.ascontiguousarray(out.reshape(4, L, L))


# version 09 (final)
# speedup vs baseline: 1.1012x; 1.1012x over previous
import numpy as np
import ml_dtypes
import concourse.bass as bass
import concourse.mybir as mybir
from concourse.bass_utils import run_bass_kernel_spmd
from concourse.alu_op_type import AluOpType as AL

F32 = mybir.dt.float32
BF16 = mybir.dt.bfloat16
AF = mybir.ActivationFunctionType

L = 4096
NCORE = 8
RPC = 2048
PASSES = 2
NRC = 8
EPS = 1e-5


def _host_constants():
    j = np.arange(L, dtype=np.float64)
    rim = np.tan(np.pi * j / L)
    rim[L // 2] = 2e4
    a = np.arange(64, dtype=np.float64)
    C1 = np.cos(2 * np.pi * np.outer(a, a) / 64)
    S1 = np.sin(2 * np.pi * np.outer(a, a) / 64)
    T2c = np.cos(2 * np.pi * np.outer(a, a) / L)
    T2s = np.sin(2 * np.pi * np.outer(a, a) / L)
    return {
        "rimrow": rim.astype(np.float32)[None, :],
        "rimmat": rim.astype(np.float32).reshape(64, 64),
        "c1": C1.astype(np.float32),
        "s1": S1.astype(np.float32),
        "negs1": (-S1).astype(np.float32),
        "t2c": T2c.astype(np.float32),
        "t2s": T2s.astype(np.float32),
        "i64f": np.eye(64, dtype=np.float32),
        "i128b": np.eye(128, dtype=ml_dtypes.bfloat16),
        "ones64": np.ones((1, 64), dtype=np.float32),
        "ones128": np.ones((1, 128), dtype=np.float32),
    }


def _build(stage=5):
    nc = bass.Bass("TRN2", target_bir_lowering=False)
    D = nc.dram_tensor

    u_in = D("u", [RPC, L], F32, kind="ExternalInput")
    ins = {}
    for nm, shp in [("lamre", [64, 1]), ("lamim", [64, 1]), ("pre", [64, 1]), ("pim", [64, 1]),
                    ("qre", [64, 1]), ("qim", [64, 1]), ("vctre", [64, 64]), ("vctim", [64, 64]),
                    ("ctre", [64, 1]), ("ctim", [64, 1]), ("bp", [64, 1]), ("dd", [1, 1]),
                    ("lst", [1, 1]), ("biasbc", [128, L]), ("coefbc", [128, 256]),
                    ("scalbc", [128, 256])]:
        ins[nm] = D(nm, shp, F32, kind="ExternalInput")
    wt_in = D("wt", [L, L], F32, kind="ExternalInput")
    at_in = D("at", [L, 256], F32, kind="ExternalInput")
    bft_in = D("bftin", [256, L], F32, kind="ExternalInput")
    HC = _host_constants()
    cst = {}
    for k, v in HC.items():
        cst[k] = D("c_" + k, list(v.shape), BF16 if v.dtype == ml_dtypes.bfloat16 else F32,
                   kind="ExternalInput")

    kpad_d = D("kpad", [4608], F32, kind="Internal")
    ard_d = D("ard", [8, L], F32, kind="Internal")
    x2s_d = D("x2s", [RPC, L], F32, kind="Internal")
    ffns_d = D("ffns", [RPC, L], BF16, kind="Internal")
    y_out = D("y", [RPC, L], F32, kind="ExternalOutput")
    if stage < 5:
        dbg_k = D("dbg_k", [4608], F32, kind="ExternalOutput")
        dbg_a = D("dbg_a", [8, L], F32, kind="ExternalOutput")
        dbg_rre = D("dbg_rre", [64, L], F32, kind="ExternalOutput")
        dbg_rim = D("dbg_rim", [64, L], F32, kind="ExternalOutput")
        dbg_w64 = D("dbg_w64", [64, 672], F32, kind="ExternalOutput")
        dbg_x2 = D("dbg_x2", [128, L], F32, kind="ExternalOutput")
        dbg_f = D("dbg_f", [128, L], BF16, kind="ExternalOutput")
        dbg_xa = D("dbg_xa", [128, 2048], BF16, kind="ExternalOutput")
        dbg_gw = D("dbg_gw", [128, 16384], BF16, kind="ExternalOutput")
        dbg_atb = D("dbg_atb", [128, 8192], BF16, kind="ExternalOutput")
        dbg_bft = D("dbg_bft", [128, 8192], BF16, kind="ExternalOutput")
        dbg_xt = D("dbg_xt", [128, 32768], BF16, kind="ExternalOutput")

    from contextlib import ExitStack
    with ExitStack() as es:
        def sbt(name, shape, dt=F32):
            return es.enter_context(nc.sbuf_tensor(name, shape, dt))
        xt = sbt("xt", [128, 32 * 1024], BF16)
        gw = sbt("gw", [128, 16384], BF16)
        bft = sbt("bft", [128, 8192], BF16)
        atb = sbt("atb", [128, 8192], BF16)
        arena = sbt("arena", [128, 32768], BF16)
        stg = sbt("stg", [128, 512], BF16)
        i128b = sbt("i128b", [128, 128], BF16)
        i64f = sbt("i64f", [64, 64], F32)
        c1 = sbt("c1", [64, 64], F32)
        s1 = sbt("s1", [64, 64], F32)
        negs1 = sbt("negs1", [64, 64], F32)
        t2c = sbt("t2c", [64, 64], F32)
        t2s = sbt("t2s", [64, 64], F32)
        rimmat = sbt("rimmat", [64, 64], F32)
        ones64 = sbt("ones64", [1, 64], F32)
        ones128 = sbt("ones128", [1, 128], F32)
        lam_c = sbt("lam_c", [64, 2], F32)
        pq_c = sbt("pq_c", [64, 4], F32)
        ct_c = sbt("ct_c", [64, 2], F32)
        bp_c = sbt("bp_c", [64, 1], F32)
        vct = sbt("vct", [64, 128], F32)
        dl_c = sbt("dl_c", [1, 2], F32)
        w64 = sbt("w64", [64, 672], F32)
        kk = sbt("kk", [64, 704], F32)
        stats = sbt("stats", [128, 64], F32)
        glob = sbt("glob", [128, 2], F32)
        coefb = sbt("coefb", [128, 256], F32)
        scalb = sbt("scalb", [128, 256], F32)
        wbcast = sbt("wbcast", [128, 256], BF16)
        zeros4 = sbt("zeros4", [128, 64], F32)
        epsc = sbt("epsc", [128, 1], F32)
        v8m = sbt("v8m", [64, 16], F32)
        bnb = sbt("bnb", [128, 128], F32)

        pa0 = es.enter_context(nc.psum_tensor("pa0", [128, 512], F32))
        pa1 = es.enter_context(nc.psum_tensor("pa1", [128, 512], F32))
        pt0 = es.enter_context(nc.psum_tensor("pt0", [128, 512], F32))
        pt1 = es.enter_context(nc.psum_tensor("pt1", [128, 512], F32))
        pm = es.enter_context(nc.psum_tensor("pm", [128, 512], F32))

        SEM = {}
        for s in ("sQ", "gQ", "T", "V", "S"):
            SEM[s] = es.enter_context(nc.semaphore(s))

        Q = {e: [] for e in ("sync", "gpsimd", "tensor", "vector", "scalar")}
        C = {s: 0 for s in SEM}

        def wait(eng, sem, val):
            if val and val > 0:
                Q[eng].append(lambda E, s=SEM[sem], v=val: E.wait_ge(s, v))

        def dma(qeng, out_ap, in_ap, sem):
            C[sem] += 16
            Q[qeng].append(lambda E, o=out_ap, i=in_ap, s=SEM[sem]: E.dma_start(out=o, in_=i).then_inc(s, 16))
            return C[sem]

        def op(eng, fn, inc=None):
            if inc:
                C[inc] += 1
                Q[eng].append(lambda E, f=fn, s=SEM[inc]: f(E).then_inc(s, 1))
                return C[inc]
            Q[eng].append(lambda E, f=fn: f(E))
            return None

        def mkap(base_ap, pattern, offset):
            a = base_ap.copy()
            a.ap = mybir.VecI64Pair(pattern)
            a.offset = offset
            return a

        def af32(lo, hi, parts=128):
            return arena[0:parts, lo:hi].bitcast(F32)

        def ab16(lo, hi):
            return arena[:, lo:hi]

        ubuf = [af32(0, 8192), af32(8192, 16384)]
        ybuf = af32(16384, 24576)
        lnrev = ab16(24576, 28672)
        dump = ab16(24576, 28672)
        x2lnb = ab16(28672, 32768)
        x2in = [af32(16384, 24576), af32(0, 8192)]
        ffnin = [ab16(8192, 12288), ab16(12288, 16384)]
        biasb = ab16(28672, 32768)
        xawb = ab16(24576, 26624)
        xawT = ab16(26624, 28672)
        fdump = ab16(24576, 28672)
        rimrow_s = arena[0:1, 0:8192].bitcast(F32)
        tsrimb = af32(8192, 16384, parts=64)
        denim = af32(16384, 24576, parts=64)
        d2b = af32(24576, 32768, parts=64)
        karr = arena[0:8, 8192:16384].bitcast(F32)

        def wcol(i, n=1, parts=64):
            return w64[0:parts, i:i + n]
        XRE, XIM, YRE, YIM, ZRE, ZIM, ZRT, ZIT, KMT, TMP = [wcol(16 + 64 * i, 64) for i in range(10)]
        bre = wcol(656); bim = wcol(657); tscol = wcol(658)
        lre2 = wcol(659); nlre = wcol(660); ts11 = w64[0:1, 661:662]
        kslc = lambda i: kk[0:64, 64 * i:64 * (i + 1)]
        ARRE, ARIM = kslc(8), kslc(9)
        dcol = glob[:, 0:1]

        def stat(par, i):
            return stats[:, 32 * par + i: 32 * par + i + 1]

        AI, ASQ, ALN, AEX = AF.Identity, AF.Square, AF.Ln, AF.Exp

        for dst, src in [(lam_c[:, 0:1], ins["lamre"]), (lam_c[:, 1:2], ins["lamim"]),
                         (pq_c[:, 0:1], ins["pre"]), (pq_c[:, 1:2], ins["pim"]),
                         (pq_c[:, 2:3], ins["qre"]), (pq_c[:, 3:4], ins["qim"]),
                         (ct_c[:, 0:1], ins["ctre"]), (ct_c[:, 1:2], ins["ctim"]),
                         (bp_c[:], ins["bp"]), (vct[:, 0:64], ins["vctre"]),
                         (vct[:, 64:128], ins["vctim"]),
                         (dl_c[:, 0:1], ins["dd"]), (dl_c[:, 1:2], ins["lst"]),
                         (i128b[:], cst["i128b"]), (i64f[:], cst["i64f"]),
                         (c1[:], cst["c1"]), (s1[:], cst["s1"]), (negs1[:], cst["negs1"]),
                         (t2c[:], cst["t2c"]), (t2s[:], cst["t2s"]),
                         (rimmat[:], cst["rimmat"]), (ones64[:], cst["ones64"]),
                         (ones128[:], cst["ones128"]), (rimrow_s, cst["rimrow"]),
                         (coefb[:], ins["coefbc"]), (scalb[:], ins["scalbc"])]:
            dma("sync", dst, src.ap(), "sQ")
        ld_all = C["sQ"]
        dma("gpsimd", bft[:], mkap(bft_in.ap(), [[4096, 128], [524288, 2], [1, 4096]], 0), "gQ")
        gq_wload = dma("gpsimd", atb[:], mkap(at_in.ap(), [[256, 128], [32768, 32], [1, 256]], 0), "gQ")

        wait("scalar", "sQ", ld_all)
        op("scalar", lambda E: E.activation(out=ts11, in_=dl_c[0:1, 1:2], func=AEX, scale=-1.0))
        op("scalar", lambda E: E.mul(ts11, ts11, 2.0))
        op("scalar", lambda E: E.activation(out=lre2, in_=lam_c[:, 0:1], func=ASQ))
        op("scalar", lambda E: E.mul(nlre, lam_c[:, 0:1], -1.0))
        op("scalar", lambda E: E.drain())
        s_prep = op("scalar", lambda E: E.nop(), inc="S")
        wait("vector", "sQ", ld_all)
        op("vector", lambda E: E.memset(zeros4[:], 0.0))
        op("vector", lambda E: E.memset(epsc[:], EPS))
        v_zw = op("vector", lambda E: E.tensor_tensor(out=wbcast[:], in0=coefb[:], in1=scalb[:], op=AL.mult), inc="V")

        wait("tensor", "S", s_prep)
        wait("tensor", "sQ", ld_all)
        op("tensor", lambda E: E.matmul(pm[0:64, 0:1], ones64[:], ts11, start=True, stop=True))
        op("tensor", lambda E: E.matmul(pm[0:128, 1:2], ones128[:], dl_c[0:1, 0:1], start=True, stop=True))
        op("tensor", lambda E: E.matmul(pm[0:64, 2:3], vct[:, 0:64], bp_c[:], start=True, stop=True))
        t_b = op("tensor", lambda E: E.matmul(pm[0:64, 3:4], vct[:, 64:128], bp_c[:], start=True, stop=True), inc="T")
        wait("scalar", "T", t_b)
        op("scalar", lambda E: E.copy(tscol, pm[0:64, 0:1]))
        op("scalar", lambda E: E.copy(dcol, pm[0:128, 1:2]))
        op("scalar", lambda E: E.copy(bre, pm[0:64, 2:3]))
        op("scalar", lambda E: E.copy(bim, pm[0:64, 3:4]))
        nBre = wcol(663); nPre = wcol(664)
        pA = wcol(665); pB = wcol(666)
        pAB = w64[0:64, 665:667]
        pDump = w64[0:64, 667:669]
        op("scalar", lambda E: E.mul(nBre, bre, -1.0))
        op("scalar", lambda E: E.mul(nPre, pq_c[:, 0:1], -1.0))
        pairs = [(ct_c[:, 0:1], ct_c[:, 1:2], bre, bim, nBre),
                 (ct_c[:, 0:1], ct_c[:, 1:2], pq_c[:, 0:1], pq_c[:, 1:2], nPre),
                 (pq_c[:, 2:3], pq_c[:, 3:4], bre, bim, nBre),
                 (pq_c[:, 2:3], pq_c[:, 3:4], pq_c[:, 0:1], pq_c[:, 1:2], nPre)]
        for k2, (are, aim, br_, bi_, nbr_) in enumerate(pairs):
            cre, cim = 2 * k2, 2 * k2 + 1
            op("scalar", lambda E, a=are, b=br_: E.activation(out=pA, in_=a, func=AI, scale=b))
            op("scalar", lambda E, a=aim, b=bi_: E.activation(out=pB, in_=a, func=AI, scale=b))
            op("scalar", lambda E, c=cre: E.activation(out=pDump, in_=pAB, func=AI, accum_out=v8m[:, c:c + 1]))
            op("scalar", lambda E, a=are, b=bi_: E.activation(out=pA, in_=a, func=AI, scale=b))
            op("scalar", lambda E, a=aim, b=nbr_: E.activation(out=pB, in_=a, func=AI, scale=b))
            op("scalar", lambda E, c=cim: E.activation(out=pDump, in_=pAB, func=AI, accum_out=v8m[:, c:c + 1]))
            op("scalar", lambda E, d=8 + 2 * k2, c=cim: E.mul(v8m[:, d:d + 1], v8m[:, c:c + 1], -1.0))
            op("scalar", lambda E, d=9 + 2 * k2, c=cre: E.copy(v8m[:, d:d + 1], v8m[:, c:c + 1]))
        op("scalar", lambda E: E.drain())
        s_v8 = op("scalar", lambda E: E.nop(), inc="S")

        s_tsr = s_v8
        for cix in range(8):
            wait("tensor", "S", s_tsr)
            t_n = op("tensor", lambda E, c=cix: E.matmul(pm[0:64, 0:512], ones64[:], rimrow_s[0:1, 512 * c:512 * (c + 1)], start=True, stop=True), inc="T")
            wait("scalar", "T", t_n)
            s_tsr = op("scalar", lambda E, c=cix: E.activation(out=tsrimb[:, 512 * c:512 * (c + 1)], in_=pm[0:64, 0:512], func=AI, scale=tscol), inc="S")
        wait("vector", "S", s_tsr)
        op("vector", lambda E: E.tensor_scalar(out=denim, in0=tsrimb, scalar1=lam_c[:, 1:2], scalar2=None, op0=AL.subtract))
        op("vector", lambda E: E.tensor_tensor(out=d2b, in0=denim, in1=denim, op=AL.mult))
        v_d2 = op("vector", lambda E: E.tensor_scalar(out=d2b, in0=d2b, scalar1=lre2, scalar2=None, op0=AL.add), inc="V")
        wait("scalar", "V", v_d2)
        op("scalar", lambda E: E.activation(out=tsrimb, in_=d2b, func=ALN))
        s_rcp = op("scalar", lambda E: E.activation(out=tsrimb, in_=tsrimb, func=AEX, scale=-1.0), inc="S")
        wait("vector", "S", s_rcp)
        op("vector", lambda E: E.tensor_scalar(out=d2b, in0=tsrimb, scalar1=nlre, scalar2=None, op0=AL.mult))
        v_r = op("vector", lambda E: E.scalar_tensor_tensor(out=denim, in0=denim, scalar=-1.0, in1=tsrimb, op0=AL.mult, op1=AL.mult), inc="V")
        wait("tensor", "V", v_r)
        v_kev = 0
        for cix in range(8):
            wait("tensor", "V", v_kev)
            op("tensor", lambda E, c=cix: E.matmul(pm[0:8, 0:512], v8m[:, 0:8], d2b[:, 512 * c:512 * (c + 1)], start=True, stop=False))
            t_n = op("tensor", lambda E, c=cix: E.matmul(pm[0:8, 0:512], v8m[:, 8:16], denim[:, 512 * c:512 * (c + 1)], start=False, stop=True), inc="T")
            wait("vector", "T", t_n)
            v_kev = op("vector", lambda E, c=cix: E.tensor_copy(karr[:, 512 * c:512 * (c + 1)], pm[0:8, 0:512]), inc="V")
        wait("sync", "V", v_kev)
        sq_karr = dma("sync", ard_d.ap(), karr, "sQ")
        wait("sync", "sQ", sq_karr)
        n = 0
        for comp in range(8):
            n = dma("sync", kslc(comp), mkap(ard_d.ap(), [[64, 64], [1, 64]], comp * 4096), "sQ")
        wait("vector", "sQ", n)
        K00R, K00I, K01R, K01I, K10R, K10I, K11R, K11I = [kslc(i) for i in range(8)]
        TMP2 = kslc(10)
        op("vector", lambda E: E.tensor_tensor(out=ARRE, in0=K01R, in1=K10R, op=AL.mult))
        op("vector", lambda E: E.tensor_tensor(out=ARIM, in0=K01I, in1=K10I, op=AL.mult))
        op("vector", lambda E: E.tensor_tensor(out=ARRE, in0=ARRE, in1=ARIM, op=AL.subtract))
        op("vector", lambda E: E.tensor_tensor(out=ARIM, in0=K01R, in1=K10I, op=AL.mult))
        op("vector", lambda E: E.tensor_tensor(out=K01R, in0=K01I, in1=K10R, op=AL.mult))
        op("vector", lambda E: E.tensor_tensor(out=ARIM, in0=ARIM, in1=K01R, op=AL.add))
        op("vector", lambda E: E.tensor_scalar(out=K11R, in0=K11R, scalar1=1.0, scalar2=None, op0=AL.add))
        op("vector", lambda E: E.tensor_tensor(out=K01R, in0=K11R, in1=K11R, op=AL.mult))
        op("vector", lambda E: E.tensor_tensor(out=K01I, in0=K11I, in1=K11I, op=AL.mult))
        v_den = op("vector", lambda E: E.tensor_tensor(out=K01R, in0=K01R, in1=K01I, op=AL.add), inc="V")
        wait("scalar", "V", v_den)
        op("scalar", lambda E: E.activation(out=TMP2, in_=K01R, func=ALN))
        s_rcp2 = op("scalar", lambda E: E.activation(out=K01I, in_=TMP2, func=AEX, scale=-1.0), inc="S")
        wait("vector", "S", s_rcp2)
        op("vector", lambda E: E.tensor_tensor(out=K10R, in0=ARRE, in1=K11R, op=AL.mult))
        op("vector", lambda E: E.tensor_tensor(out=TMP2, in0=ARIM, in1=K11I, op=AL.mult))
        op("vector", lambda E: E.tensor_tensor(out=K10R, in0=K10R, in1=TMP2, op=AL.add))
        op("vector", lambda E: E.tensor_tensor(out=K10R, in0=K10R, in1=K01I, op=AL.mult))
        op("vector", lambda E: E.tensor_tensor(out=TMP2, in0=ARIM, in1=K11R, op=AL.mult))
        op("vector", lambda E: E.tensor_tensor(out=K11I, in0=ARRE, in1=K11I, op=AL.mult))
        op("vector", lambda E: E.tensor_tensor(out=TMP2, in0=TMP2, in1=K11I, op=AL.subtract))
        op("vector", lambda E: E.tensor_tensor(out=TMP2, in0=TMP2, in1=K01I, op=AL.mult))
        op("vector", lambda E: E.tensor_tensor(out=K00R, in0=K00R, in1=K10R, op=AL.subtract))
        op("vector", lambda E: E.tensor_tensor(out=K00I, in0=K00I, in1=TMP2, op=AL.subtract))
        op("vector", lambda E: E.tensor_tensor(out=ARRE, in0=rimmat[:], in1=K00I, op=AL.mult))
        op("vector", lambda E: E.tensor_tensor(out=ARRE, in0=K00R, in1=ARRE, op=AL.subtract))
        op("vector", lambda E: E.tensor_tensor(out=ARIM, in0=rimmat[:], in1=K00R, op=AL.mult))
        v_ar = op("vector", lambda E: E.tensor_tensor(out=ARIM, in0=ARIM, in1=K00I, op=AL.add), inc="V")
        wait("tensor", "V", v_ar)
        op("tensor", lambda E: E.matmul(pm[0:64, 0:64], c1[:], ARRE, start=True, stop=False))
        op("tensor", lambda E: E.matmul(pm[0:64, 0:64], negs1[:], ARIM, start=False, stop=True))
        op("tensor", lambda E: E.matmul(pm[0:64, 64:128], s1[:], ARRE, start=True, stop=False))
        t_n = op("tensor", lambda E: E.matmul(pm[0:64, 64:128], c1[:], ARIM, start=False, stop=True), inc="T")
        wait("vector", "T", t_n)
        op("vector", lambda E: E.tensor_copy(YRE, pm[0:64, 0:64]))
        op("vector", lambda E: E.tensor_copy(YIM, pm[0:64, 64:128]))
        op("vector", lambda E: E.tensor_tensor(out=ZRE, in0=YRE, in1=t2c[:], op=AL.mult))
        op("vector", lambda E: E.tensor_tensor(out=TMP, in0=YIM, in1=t2s[:], op=AL.mult))
        op("vector", lambda E: E.tensor_tensor(out=ZRE, in0=ZRE, in1=TMP, op=AL.subtract))
        op("vector", lambda E: E.tensor_tensor(out=ZIM, in0=YRE, in1=t2s[:], op=AL.mult))
        op("vector", lambda E: E.tensor_tensor(out=TMP, in0=YIM, in1=t2c[:], op=AL.mult))
        v_tw = op("vector", lambda E: E.tensor_tensor(out=ZIM, in0=ZIM, in1=TMP, op=AL.add), inc="V")
        wait("tensor", "V", v_tw)
        op("tensor", lambda E: E.matmul(pm[0:64, 128:192], ZRE, i64f[:], start=True, stop=True))
        t_n = op("tensor", lambda E: E.matmul(pm[0:64, 192:256], ZIM, i64f[:], start=True, stop=True), inc="T")
        wait("vector", "T", t_n)
        op("vector", lambda E: E.tensor_copy(ZRT, pm[0:64, 128:192]))
        v_n = op("vector", lambda E: E.tensor_copy(ZIT, pm[0:64, 192:256]), inc="V")
        wait("tensor", "V", v_n)
        op("tensor", lambda E: E.matmul(pm[0:64, 256:320], ZRT, c1[:], start=True, stop=False))
        t_n = op("tensor", lambda E: E.matmul(pm[0:64, 256:320], ZIT, negs1[:], start=False, stop=True), inc="T")
        wait("vector", "T", t_n)
        v_n = op("vector", lambda E: E.tensor_copy(KMT, pm[0:64, 256:320]), inc="V")
        wait("tensor", "V", v_n)
        t_kprep_end = op("tensor", lambda E: E.matmul(pm[0:64, 320:384], KMT, i64f[:], start=True, stop=True), inc="T")
        wait("scalar", "T", t_kprep_end)
        s_km = op("scalar", lambda E: E.activation(out=TMP, in_=pm[0:64, 320:384], func=AI, scale=1.0 / 4096.0), inc="S")
        wait("sync", "S", s_km)
        wait("sync", "V", v_zw)
        dma("sync", mkap(kpad_d.ap(), [[4, 128], [1, 4]], 0), zeros4[:, 0:4], "sQ")
        sq_kpad = dma("sync", mkap(kpad_d.ap(), [[64, 64], [1, 64]], 512), TMP, "sQ")

        def emit_g_load():
            wait("gpsimd", "sQ", sq_kpad)
            last = None
            for d_ in range(8):
                last = dma("gpsimd", gw[:, 2048 * d_:2048 * (d_ + 1)],
                           mkap(kpad_d.ap(), [[1, 128], [128, 4], [1, 512]], 512 * d_ + 1), "gQ")
            return last

        gq_g = emit_g_load()

        t_pt_free = {0: 0, 1: 0}
        t_pa_free = {0: 0, 1: 0}
        u_free = {0: 0, 1: 0}
        sq_x2store = 0
        t_x2trans = {}
        state = {"pmfree": 0}

        def emit_lnstats(par, x_ap, dmp_ap):
            b = 16 * par
            col = lambda i, n=1: bnb[:, b + i:b + i + n]
            cQ, cS, cM = col(0), col(1), col(2)
            v_st = op("vector", lambda E: E.tensor_reduce(
                out=cS, in_=x_ap, axis=mybir.AxisListType.X, op=AL.add), inc="V")
            wait("scalar", "V", v_st)
            op("scalar", lambda E: E.activation(out=dmp_ap, in_=x_ap, func=ASQ, accum_out=cQ))
            op("scalar", lambda E: E.mul(cM, cS, 1.0 / L))
            op("scalar", lambda E: E.activation(out=col(3), in_=cM, func=ASQ))
            op("scalar", lambda E: E.mul(col(4), cQ, 1.0 / L))
            op("scalar", lambda E: E.activation(out=col(5), in_=col(3), func=AI, scale=-1.0, bias=epsc[:]))
            op("scalar", lambda E: E.activation(out=col(10, 2), in_=col(4, 2), func=AI, accum_out=col(6)))
            op("scalar", lambda E: E.activation(out=col(7), in_=col(6), func=ALN))
            op("scalar", lambda E: E.activation(out=col(8), in_=col(7), func=AEX, scale=-0.5))
            op("scalar", lambda E: E.drain())
            s_done = op("scalar", lambda E: E.nop(), inc="S")
            return s_done, cM, col(8)

        def conv_rc(p, rc):
            nonlocal sq_x2store
            grc = p * NRC + rc
            par = grc % 2
            row = grc * 128
            wait("sync", "V", u_free[par])
            if grc <= 1:
                wait("sync", "T", t_kprep_end)
                if grc == 1:
                    wait("sync", "sQ", sq_karr)
            sq_u = dma("sync", ubuf[par], u_in.ap()[row:row + 128, :], "sQ")
            wait("vector", "sQ", sq_u)
            s_ln1, m1, sc1 = emit_lnstats(par, ubuf[par], dump[:])
            wait("vector", "S", s_ln1)
            if grc == 0:
                wait("vector", "T", t_kprep_end)
            v_ln = op("vector", lambda E, m=m1, s=sc1: E.tensor_scalar(
                out=lnrev[:], in0=ubuf[par][:, ::-1], scalar1=m, scalar2=s,
                op0=AL.subtract, op1=AL.mult), inc="V")
            wait("tensor", "V", v_ln)
            v_xtrdy = 0
            for g in range(8):
                bank = [pt0, pt1][g % 2]
                wait("tensor", "V", t_pt_free[g % 2])
                t_g = None
                for kp in range(4):
                    b_blk = 4 * g + 3 - kp
                    t_g = op("tensor", lambda E, b=b_blk, k=kp, bk=bank: E.matmul(
                        bk[:, 128 * k:128 * (k + 1)], lnrev[:, 128 * b:128 * (b + 1)], i128b[:],
                        start=True, stop=True), inc=("T" if kp == 3 else None))
                wait("vector", "T", t_g)
                dst = mkap(xt[:], [[32768, 128], [1024, 4], [1, 128]], (28 - 4 * g) * 1024 + rc * 128)
                srcv = mkap(bank[:], [[512, 128], [128, 4], [1, 128]], 0)
                t_pt_free[g % 2] = op("vector", lambda E, d=dst, s=srcv: E.tensor_copy(d, s), inc="V")
                v_xtrdy = t_pt_free[g % 2]
            wait("tensor", "V", v_xtrdy)
            if grc == 0:
                wait("tensor", "gQ", gq_g)
            v_yb = 0
            for i in range(8):
                bank = [pa0, pa1][i % 2]
                wait("tensor", "V", t_pa_free[i % 2])
                nmm = 4 * (i + 1)
                cnt = 0
                t_i = None
                for j in range(i + 1):
                    d_ = i - j
                    for bt in range(4):
                        cnt += 1
                        sl = 4 * j + bt
                        t_i = op("tensor", lambda E, s_=sl, dd_=d_, b=bt, bk=bank, st=(cnt == 1), sp=(cnt == nmm): E.matmul(
                            bk[:, 0:512], xt[:, s_ * 1024 + rc * 128: s_ * 1024 + rc * 128 + 128],
                            gw[:, dd_ * 2048 + (3 - b) * 512: dd_ * 2048 + (3 - b) * 512 + 512],
                            start=st, stop=sp), inc=("T" if cnt == nmm else None))
                wait("vector", "T", t_i)
                if i == 0:
                    wait("vector", "sQ", sq_x2store)
                stop_ = 3583 - 512 * i
                sl_ln = lnrev[:, 4095 - 512 * i: (stop_ if stop_ >= 0 else None): -1]
                t_pa_free[i % 2] = op("vector", lambda E, b=bank, s=sl_ln, i_=i: E.scalar_tensor_tensor(
                    out=ybuf[:, 512 * i_:512 * (i_ + 1)], in0=s, scalar=dcol, in1=b[:, 0:512],
                    op0=AL.mult, op1=AL.add), inc="V")
                v_yb = t_pa_free[i % 2]
            s_pl, m2, sc2 = emit_lnstats(par, ybuf, dump[:])
            wait("vector", "S", s_pl)
            op("vector", lambda E, m=m2, s=sc2: E.tensor_scalar(
                out=ybuf, in0=ybuf, scalar1=m, scalar2=s, op0=AL.subtract, op1=AL.mult))
            v_x2 = op("vector", lambda E: E.tensor_tensor(out=ybuf, in0=ybuf, in1=ubuf[par], op=AL.add), inc="V")
            u_free[par] = v_x2
            s_l2, m3, sc3 = emit_lnstats(par, ybuf, dump[:])
            wait("vector", "S", s_l2)
            if grc >= 1:
                wait("vector", "T", t_x2trans.get(grc - 1, 0))
            v_x2ln = op("vector", lambda E, m=m3, s=sc3: E.tensor_scalar(
                out=x2lnb[:], in0=ybuf, scalar1=m, scalar2=s,
                op0=AL.subtract, op1=AL.mult), inc="V")
            wait("sync", "V", v_x2ln)
            sq_x2store = dma("sync", x2s_d.ap()[row:row + 128, :], ybuf, "sQ")
            wait("tensor", "V", v_x2ln)
            t_last = None
            for g in range(8):
                bank = [pt0, pt1][g % 2]
                wait("tensor", "V", t_pt_free[g % 2])
                t_g = None
                for kp in range(4):
                    b_blk = 4 * g + kp
                    t_g = op("tensor", lambda E, b=b_blk, k=kp, bk=bank: E.matmul(
                        bk[:, 128 * k:128 * (k + 1)], x2lnb[:, 128 * b:128 * (b + 1)], i128b[:],
                        start=True, stop=True), inc=("T" if kp == 3 else None))
                t_last = t_g
                wait("vector", "T", t_g)
                dst = mkap(xt[:], [[32768, 128], [1024, 4], [1, 128]], (4 * g) * 1024 + rc * 128)
                srcv = mkap(bank[:], [[512, 128], [128, 4], [1, 128]], 0)
                t_pt_free[g % 2] = op("vector", lambda E, d=dst, s=srcv: E.tensor_copy(d, s), inc="V")
            t_x2trans[grc] = t_last
            return t_last

        t_ffn_end = 0
        v_ffnend = 0
        s_ffnend = 0
        npass = 0 if stage <= 1 else (1 if stage < 5 else PASSES)
        for p in range(npass):
            if p > 0:
                wait("gpsimd", "T", t_ffn_end)
                gq_gload = emit_g_load()
                sq_x2store = C["sQ"]
                wait("sync", "V", v_ffnend)
                wait("sync", "S", s_ffnend)
                wait("sync", "sQ", C["sQ"])
                wait("scalar", "V", v_ffnend)
                wait("vector", "S", s_ffnend)
                wait("vector", "T", t_ffn_end)
                wait("tensor", "V", v_ffnend)
                wait("tensor", "gQ", gq_gload)
            t_convdone = 0
            for rc in range(NRC):
                t_convdone = conv_rc(p, rc)
            v_convdone = C["V"]
            s_convdone = C["S"]
            sq_convdone = C["sQ"]

            if stage < 3:
                continue
            wait("gpsimd", "T", t_convdone)
            wait("gpsimd", "V", v_convdone)
            wait("gpsimd", "S", s_convdone)
            gq_bias = dma("gpsimd", biasb[:], ins["biasbc"].ap(), "gQ")
            if p == 0:
                wait("tensor", "gQ", gq_wload)
            v_xaw = {}
            for rc in range(NRC):
                wait("tensor", "V", state["pmfree"])
                t_xa = None
                for dt_ in range(32):
                    t_xa = op("tensor", lambda E, d=dt_, r=rc, st=(dt_ == 0), sp=(dt_ == 31): E.matmul(
                        pm[:, 0:256], xt[:, d * 1024 + r * 128: d * 1024 + r * 128 + 128],
                        atb[:, d * 256:(d + 1) * 256], start=st, stop=sp), inc=("T" if dt_ == 31 else None))
                wait("vector", "T", t_xa)
                if rc == 0:
                    wait("vector", "S", s_convdone)
                state["pmfree"] = op("vector", lambda E, r=rc: E.tensor_tensor(
                    out=xawb[:, r * 256:(r + 1) * 256], in0=pm[:, 0:256], in1=wbcast[:], op=AL.mult), inc="V")
                v_xaw[rc] = state["pmfree"]
            for rc in range(NRC):
                wait("tensor", "V", v_xaw[rc])
                for t_ in range(2):
                    bank = [pt0, pt1][t_]
                    wait("tensor", "V", t_pt_free[t_])
                    t_g = op("tensor", lambda E, r=rc, t=t_, bk=bank: E.matmul(
                        bk[:, 0:128], xawb[:, r * 256 + t * 128: r * 256 + t * 128 + 128], i128b[:],
                        start=True, stop=True), inc="T")
                    wait("vector", "T", t_g)
                    t_pt_free[t_] = op("vector", lambda E, r=rc, t=t_, bk=bank: E.tensor_copy(
                        xawT[:, t * 1024 + r * 128: t * 1024 + r * 128 + 128], bk[:, 0:128]), inc="V")
            t_slice_mm = {}
            gq_stg_free = {0: 0, 1: 0}
            for k in range(16):
                wait("gpsimd", "T", t_slice_mm.get(k - 2, t_convdone))
                gq_wt = dma("gpsimd", gw[:, (k % 2) * 8192:(k % 2) * 8192 + 8192],
                            mkap(wt_in.ap(), [[4096, 128], [524288, 32], [1, 256]], k * 256), "gQ")
                wait("tensor", "gQ", gq_wt)
                if k == 0:
                    wait("tensor", "gQ", gq_bias)
                for rc in range(NRC):
                    bank = [pa0, pa1][rc % 2]
                    par = rc % 2
                    row = (p * NRC + rc) * 128
                    wait("tensor", "V", t_pa_free[par])
                    for dt_ in range(32):
                        op("tensor", lambda E, d=dt_, r=rc, kk_=k, bk=bank, st=(dt_ == 0): E.matmul(
                            bk[:, 0:256], xt[:, d * 1024 + r * 128: d * 1024 + r * 128 + 128],
                            gw[:, (kk_ % 2) * 8192 + d * 256:(kk_ % 2) * 8192 + (d + 1) * 256],
                            start=st, stop=False))
                    t_m = None
                    for t_ in range(2):
                        t_m = op("tensor", lambda E, t=t_, r=rc, kk_=k, bk=bank: E.matmul(
                            bk[:, 0:256], xawT[:, t * 1024 + r * 128: t * 1024 + r * 128 + 128],
                            bft[:, t * 4096 + kk_ * 256: t * 4096 + (kk_ + 1) * 256],
                            start=False, stop=(t == 1)), inc=("T" if t_ == 1 else None))
                    if rc == NRC - 1:
                        t_slice_mm[k] = t_m
                    wait("vector", "T", t_m)
                    wait("vector", "gQ", gq_stg_free[par])
                    v_ev = op("vector", lambda E, kk_=k, bk=bank, pp=par: E.tensor_tensor(
                        out=stg[:, pp * 256:(pp + 1) * 256], in0=bk[:, 0:256],
                        in1=biasb[:, kk_ * 256:(kk_ + 1) * 256], op=AL.add), inc="V")
                    t_pa_free[par] = v_ev
                    wait("gpsimd", "V", v_ev)
                    gq_stg_free[par] = dma("gpsimd", ffns_d.ap()[row:row + 128, k * 256:(k + 1) * 256],
                                           stg[:, par * 256:(par + 1) * 256], "gQ")
            t_ffn_end = t_slice_mm[15]
            gq_allst = C["gQ"]
            if stage < 4:
                continue
            v_fin = {0: C["V"], 1: C["V"]}
            tbig = af32(24576, 32768)
            v_first = True
            for rc in range(NRC):
                par = rc % 2
                row = (p * NRC + rc) * 128
                wait("sync", "V", v_fin[par])
                wait("sync", "sQ", sq_convdone)
                sq_x2 = dma("sync", x2in[par], x2s_d.ap()[row:row + 128, :], "sQ")
                wait("gpsimd", "gQ", gq_allst)
                wait("gpsimd", "V", v_fin[par])
                gq_f = dma("gpsimd", ffnin[par], ffns_d.ap()[row:row + 128, :], "gQ")
                wait("vector", "gQ", gq_f)
                if v_first:
                    wait("vector", "T", t_ffn_end)
                    v_first = False
                s_f, mf, sf = emit_lnstats(par, ffnin[par], fdump[:])
                wait("vector", "S", s_f)
                wait("vector", "sQ", sq_x2)
                op("vector", lambda E, pp=par, m=mf, s=sf: E.tensor_scalar(
                    out=tbig, in0=ffnin[pp], scalar1=m, scalar2=s,
                    op0=AL.subtract, op1=AL.mult))
                v_done = op("vector", lambda E, pp=par: E.tensor_tensor(
                    out=x2in[pp], in0=x2in[pp], in1=tbig, op=AL.add), inc="V")
                v_fin[par] = v_done
                wait("sync", "V", v_done)
                dma("sync", y_out.ap()[row:row + 128, :], x2in[par], "sQ")
            v_ffnend = C["V"]
            s_ffnend = C["S"]

        if stage < 5:
            wait("sync", "sQ", C["sQ"])
            dma("sync", dbg_k.ap(), kpad_d.ap(), "sQ")
            dma("sync", dbg_a.ap(), ard_d.ap(), "sQ")
            if stage == 1:
                dma("sync", dbg_rre.ap(), d2b, "sQ")
                dma("sync", dbg_rim.ap(), denim, "sQ")
                dma("sync", dbg_w64.ap(), w64[:], "sQ")
            if stage >= 2:
                dma("sync", dbg_x2.ap(), x2s_d.ap()[0:128, :], "sQ")
            if stage >= 3:
                wait("sync", "gQ", C["gQ"])
                dma("sync", dbg_f.ap(), ffns_d.ap()[0:128, :], "sQ")
                dma("sync", dbg_xa.ap(), xawb, "sQ")
                dma("sync", dbg_gw.ap(), gw[:], "sQ")
                dma("sync", dbg_atb.ap(), atb[:], "sQ")
                dma("sync", dbg_bft.ap(), bft[:], "sQ")
                dma("sync", dbg_xt.ap(), xt[:], "sQ")
        wait("sync", "sQ", C["sQ"])

        with nc.Block() as block:
            @block.sync
            def _(E):
                for fn in Q["sync"]:
                    fn(E)

            @block.gpsimd
            def _(E):
                for fn in Q["gpsimd"]:
                    fn(E)

            @block.tensor
            def _(E):
                for fn in Q["tensor"]:
                    fn(E)

            @block.vector
            def _(E):
                for fn in Q["vector"]:
                    fn(E)

            @block.scalar
            def _(E):
                for fn in Q["scalar"]:
                    fn(E)

    return nc


def _prep_common(L0_attn_Lambda_ri, L0_attn_p_ri, L0_attn_q_ri, L0_attn_Vc_ri,
                 L0_attn_Ct_ri, L0_attn_Bp, L0_attn_D, L0_attn_log_step, L0_ffn_W,
                 L0_ffn_bias, L0_ffn_A, L0_ffn_Bf, L0_ffn_coeffs, L0_ffn_scales):
    f32 = np.float32
    lam = np.asarray(L0_attn_Lambda_ri, f32)
    pp = np.asarray(L0_attn_p_ri, f32)
    qq = np.asarray(L0_attn_q_ri, f32)
    vc = np.asarray(L0_attn_Vc_ri, f32)
    ct = np.asarray(L0_attn_Ct_ri, f32).reshape(64, 2)
    common = {
        "lamre": np.ascontiguousarray(lam[:, 0:1]),
        "lamim": np.ascontiguousarray(lam[:, 1:2]),
        "pre": np.ascontiguousarray(pp[:, 0:1]),
        "pim": np.ascontiguousarray(pp[:, 1:2]),
        "qre": np.ascontiguousarray(qq[:, 0:1]),
        "qim": np.ascontiguousarray(qq[:, 1:2]),
        "vctre": np.ascontiguousarray(vc[:, :, 0].T),
        "vctim": np.ascontiguousarray(vc[:, :, 1].T),
        "ctre": np.ascontiguousarray(ct[:, 0:1]),
        "ctim": np.ascontiguousarray(ct[:, 1:2]),
        "bp": np.ascontiguousarray(np.asarray(L0_attn_Bp, f32)),
        "dd": np.asarray(L0_attn_D, f32).reshape(1, 1),
        "lst": np.asarray(L0_attn_log_step, f32).reshape(1, 1),
        "wt": np.ascontiguousarray(np.asarray(L0_ffn_W, f32).T),
        "at": np.ascontiguousarray(np.asarray(L0_ffn_A, f32).reshape(256, L).T),
        "bftin": np.ascontiguousarray(np.transpose(np.asarray(L0_ffn_Bf, f32), (0, 2, 1)).reshape(256, L)),
        "biasbc": np.ascontiguousarray(np.broadcast_to(np.asarray(L0_ffn_bias, f32)[None, :], (128, L))),
        "coefbc": np.ascontiguousarray(np.broadcast_to(
            np.asarray(L0_ffn_coeffs, f32)[np.repeat(np.arange(16), 16)][None, :], (128, 256))),
        "scalbc": np.ascontiguousarray(np.broadcast_to(
            np.asarray(L0_ffn_scales, f32)[np.repeat(np.arange(16), 16)][None, :], (128, 256))),
    }
    for k, v in _host_constants().items():
        common["c_" + k] = v
    return common


def kernel(L0_attn_u, L0_attn_Lambda_ri, L0_attn_p_ri, L0_attn_q_ri, L0_attn_Vc_ri,
           L0_attn_Ct_ri, L0_attn_Bp, L0_attn_D, L0_attn_log_step, L0_ffn_W,
           L0_ffn_bias, L0_ffn_A, L0_ffn_Bf, L0_ffn_coeffs, L0_ffn_scales):
    u = np.ascontiguousarray(np.asarray(L0_attn_u, np.float32).reshape(16384, L))
    common = _prep_common(L0_attn_Lambda_ri, L0_attn_p_ri, L0_attn_q_ri, L0_attn_Vc_ri,
                          L0_attn_Ct_ri, L0_attn_Bp, L0_attn_D, L0_attn_log_step,
                          L0_ffn_W, L0_ffn_bias, L0_ffn_A, L0_ffn_Bf, L0_ffn_coeffs,
                          L0_ffn_scales)
    nc = _build()
    in_maps = []
    for c in range(NCORE):
        m = dict(common)
        m["u"] = np.ascontiguousarray(u[c * RPC:(c + 1) * RPC])
        in_maps.append(m)
    res = run_bass_kernel_spmd(nc, in_maps, core_ids=list(range(NCORE)))
    out = np.concatenate([np.asarray(r["y"]) for r in res.results], axis=0)
    return np.ascontiguousarray(out.reshape(4, L, L))
